# version 01 (final)
# baseline (speedup 1.0000x reference)
import numpy as np
import ml_dtypes
from contextlib import ExitStack

import concourse.bass as bass
import concourse.mybir as mybir
from concourse.bass_utils import run_bass_kernel_spmd
from concourse.ap import AP

F32 = mybir.dt.float32
BF16 = mybir.dt.bfloat16
F16 = mybir.dt.float16
I32 = mybir.dt.int32
ALU = mybir.AluOpType
AFT = mybir.ActivationFunctionType
AX = mybir.AxisListType

N_CORES = 8
B, L, D = 16, 2048, 1024
BC = B // N_CORES
T = BC * L
NT = T // 128
NTS = L // 128
G = 4
DG = D // G
WS = [4, 16, 64, 256]
INNER = 4096
QSCALE = 2.0 / float(np.sqrt(L))
EPS = 1e-5

N_SUB = 4
AA_STAGE = 4
ENGS = ("tensor", "vector", "scalar", "gpsimd", "sync")


class Sem:
    def __init__(self, es, nc, name):
        self.h = es.enter_context(nc.semaphore(name))
        self.n = 0

    def mark(self, k=1):
        self.n += k
        return self.n


def ln_chain(v, a_or_none, x_ap, out_ap, sc, eng_phase):
    st, ag, tmp = sc["st"], sc["ag"], sc["tmp"]
    if eng_phase == "v1":
        v.bn_stats(st[:, 0, :], x_ap[:, 0:512])
        v.bn_stats(st[:, 1, :], x_ap[:, 512:1024])
        v.drain()
        v.bn_aggr(ag[:, 0, :], st[:, 0, :])
        v.bn_aggr(ag[:, 1, :], st[:, 1, :])
        v.drain()
        m1, v1 = ag[:, 0, 0:1], ag[:, 0, 1:2]
        m2, v2 = ag[:, 1, 0:1], ag[:, 1, 1:2]
        v.tensor_sub(tmp[:, 0:1], m1, m2)
        v.tensor_add(tmp[:, 1:2], m1, m2)
        v.tensor_add(tmp[:, 3:4], v1, v2)
        v.drain()
        v.tensor_scalar(tmp[:, 2:3], tmp[:, 1:2], 0.5, None, ALU.mult)
        v.tensor_scalar(tmp[:, 4:5], tmp[:, 0:1], 0.5, None, ALU.mult)
        v.drain()
        v.scalar_tensor_tensor(tmp[:, 5:6], tmp[:, 4:5], 1.0, tmp[:, 4:5],
                               ALU.mult, ALU.mult)
        v.drain()
        v.scalar_tensor_tensor(tmp[:, 6:7], tmp[:, 3:4], 0.5, tmp[:, 5:6],
                               ALU.mult, ALU.add)
        v.drain()
        v.tensor_scalar(tmp[:, 6:7], tmp[:, 6:7], EPS, None, ALU.add)
    elif eng_phase == "a":
        a_or_none.activation(tmp[:, 7:8], tmp[:, 6:7], AFT.Sqrt)
    elif eng_phase == "v2":
        v.reciprocal(tmp[:, 7:8], tmp[:, 7:8])
        v.drain()
        v.scalar_tensor_tensor(tmp[:, 4:5], tmp[:, 7:8], 1.0, tmp[:, 7:8],
                               ALU.mult, ALU.mult)
        v.drain()
        v.tensor_mul(tmp[:, 4:5], tmp[:, 4:5], tmp[:, 6:7])
        v.drain()
        v.tensor_scalar(tmp[:, 4:5], tmp[:, 4:5], -0.5, 1.5, ALU.mult, ALU.add)
        v.drain()
        v.tensor_mul(tmp[:, 7:8], tmp[:, 7:8], tmp[:, 4:5])
        v.drain()
        v.tensor_scalar(out_ap, x_ap, tmp[:, 2:3], tmp[:, 7:8],
                        ALU.subtract, ALU.mult)
    else:
        raise ValueError(eng_phase)


class Emit:

    def __init__(self, nc, block):
        self.nc = nc
        self.block = block
        self.q = {e: [] for e in ENGS}

    def __getattr__(self, eng):
        if eng in ENGS:
            return lambda fn: self.q[eng].append(fn)
        raise AttributeError(eng)

    def flush(self):
        for eng in ENGS:
            fns = self.q[eng]
            if fns:
                getattr(self.block, eng)(lambda e, fns=tuple(fns): [f(e) for f in fns])
        self.q = {e: [] for e in ENGS}

    def barrier(self, bar):
        self.flush()
        target = bar.mark(5)
        for eng in ENGS:
            def body(e, eng=eng):
                ins = e.nop() if eng == "sync" else e.drain()
                ins.then_inc(bar.h, 1)
                e.wait_ge(bar.h, target)
            self.q[eng].append(body)
        self.flush()


def build_program(n_sub=4):
    nc = bass.Bass("TRN2", target_bir_lowering=False, debug=False,
                   detect_race_conditions=False)

    x0 = nc.dram_tensor("x0", [T, D], F32, kind="ExternalInput").ap()
    qkv_wT = nc.dram_tensor("qkv_wT", [D, 3 * D], F32, kind="ExternalInput").ap()
    qkv_b = nc.dram_tensor("qkv_b", [3 * D], F32, kind="ExternalInput").ap()
    off_w = nc.dram_tensor("off_w", [256, D], F32, kind="ExternalInput").ap()
    lin2_wT = nc.dram_tensor("lin2_wT", [256, 1], F32, kind="ExternalInput").ap()
    lin2_b = nc.dram_tensor("lin2_b", [1], F32, kind="ExternalInput").ap()
    phazor = nc.dram_tensor("phazor", [D, 2], F32, kind="ExternalInput").ap()
    phazor_init = nc.dram_tensor("phazor_init", [D, 2], F32, kind="ExternalInput").ap()
    last_re = nc.dram_tensor("last_re", [D], F32, kind="ExternalInput").ap()
    last_im = nc.dram_tensor("last_im", [D], F32, kind="ExternalInput").ap()
    wgT = nc.dram_tensor("wgT", [D, INNER], F32, kind="ExternalInput").ap()
    wuT = nc.dram_tensor("wuT", [D, INNER], F32, kind="ExternalInput").ap()
    wdT = nc.dram_tensor("wdT", [INNER, D], F32, kind="ExternalInput").ap()
    out = nc.dram_tensor("out", [T, D], F32, kind="ExternalOutput").ap()

    d1 = nc.dram_tensor("resid1", [T, D], F32, kind="Internal").ap()
    d2 = nc.dram_tensor("resid2", [T, D], F32, kind="Internal").ap()
    d3 = nc.dram_tensor("resid3", [T, D], F32, kind="Internal").ap()
    tern_g = nc.dram_tensor("tern_g", [D, INNER], BF16, kind="Internal").ap()
    tern_u = nc.dram_tensor("tern_u", [D, INNER], BF16, kind="Internal").ap()
    tern_d = nc.dram_tensor("tern_d", [INNER, D], BF16, kind="Internal").ap()
    powers_hbm = nc.dram_tensor("powers_hbm", [D, 128, 256], F32, kind="Internal").ap()
    H_hbm = nc.dram_tensor("H_hbm", [2, 32, D], F32, kind="Internal").ap()
    S_hbm = nc.dram_tensor("S_hbm", [2, 33, D], F32, kind="Internal").ap()

    dsts = {1: d1, 2: d2, 3: d3}
    dsts[n_sub] = out
    srcs = {1: x0, 2: d1, 3: d2, 4: d3}

    es = ExitStack()
    with es:
        S = {name: Sem(es, nc, name) for name in
             ("bar", "ld", "st", "wl", "gl", "gs", "pe", "ve", "ac", "gp",
              "pe2", "ve2", "ac2")}

        identB = es.enter_context(nc.sbuf_tensor("identB", [128, 128], BF16)).ap()
        identF = es.enter_context(nc.sbuf_tensor("identF", [128, 128], F32)).ap()
        Janti = es.enter_context(nc.sbuf_tensor("Janti", [128, 128], BF16)).ap()
        onesP = es.enter_context(nc.sbuf_tensor("onesP", [128, 128], BF16)).ap()
        ones_col = es.enter_context(nc.sbuf_tensor("ones_col", [128, 1], BF16)).ap()
        ones_row = es.enter_context(nc.sbuf_tensor("ones_row", [1, 128], BF16)).ap()
        ones_rf = es.enter_context(nc.sbuf_tensor("ones_rf", [1, 128], F32)).ap()
        C2 = es.enter_context(nc.sbuf_tensor("C2", [128, 4352], F16)).ap()
        U1 = es.enter_context(nc.sbuf_tensor("U1", [128, 4352], BF16)).ap()
        identH = es.enter_context(nc.sbuf_tensor("identH", [128, 128], F16)).ap()
        mw_g = es.enter_context(nc.sbuf_tensor("mw_g", [128, 1], F32)).ap()
        mw_u = es.enter_context(nc.sbuf_tensor("mw_u", [128, 1], F32)).ap()
        mw_d = es.enter_context(nc.sbuf_tensor("mw_d", [128, 1], F32)).ap()

        cn = dict(identB=identB, identF=identF, identH=identH, Janti=Janti,
                  onesP=onesP, ones_col=ones_col, ones_row=ones_row,
                  ones_rf=ones_rf, C2=C2, U1=U1)

        with nc.Block() as block:
            E = Emit(nc, block)

            tgp = S["gp"].mark()

            def i_gp(g):
                g.iota(C2, pattern=[[-1, 4352]], base=2048, channel_multiplier=1,
                       allow_small_or_imprecise_dtypes=True)
                g.memset(onesP, 1.0)
                g.memset(ones_col, 1.0)
                g.memset(ones_row, 1.0)
                g.memset(ones_rf, 1.0)
                g.affine_select(identB, onesP, pattern=[[-1, 128]],
                                compare_op=ALU.is_equal, fill=0.0, base=0,
                                channel_multiplier=1)
                g.affine_select(Janti, onesP, pattern=[[1, 128]],
                                compare_op=ALU.is_equal, fill=0.0, base=-127,
                                channel_multiplier=1)
                g.drain().then_inc(S["gp"].h, 1)
            E.gpsimd(i_gp)

            tve = S["ve"].mark()

            def i_ve(v, tgp=tgp):
                v.wait_ge(S["gp"].h, tgp)
                v.tensor_scalar(U1, C2, 0.0, None, ALU.is_ge)
                v.tensor_copy(identF, identB)
                v.tensor_copy(identH, identB)
                v.drain().then_inc(S["ve"].h, 1)
            E.vector(i_ve)

            E.barrier(S["bar"])

            emit_attn_a(nc, E, S, cn, x_src=srcs[1], dst=dsts[1],
                        qkv_wT=qkv_wT, qkv_b=qkv_b, off_w=off_w,
                        lin2_wT=lin2_wT, lin2_b=lin2_b)
            E.barrier(S["bar"])

            if n_sub >= 2:
                emit_wquant(nc, E, S, cn,
                            w_list=[(wgT, tern_g, D), (wuT, tern_u, D),
                                    (wdT, tern_d, INNER)],
                            mw_out=[mw_g, mw_u, mw_d])
                E.barrier(S["bar"])
                emit_ffn(nc, E, S, cn, src=srcs[2], dst=dsts[2],
                         tern_g=tern_g, tern_u=tern_u, tern_d=tern_d,
                         mw=(mw_g, mw_u, mw_d), tag="f1")
                E.barrier(S["bar"])

            if n_sub >= 3:
                emit_attn_b(nc, E, S, cn, src=srcs[3], dst=dsts[3],
                            phazor=phazor, phazor_init=phazor_init,
                            last_re=last_re, last_im=last_im,
                            powers_hbm=powers_hbm, H_hbm=H_hbm, S_hbm=S_hbm)
                E.barrier(S["bar"])

            if n_sub >= 4:
                emit_ffn(nc, E, S, cn, src=srcs[4], dst=dsts[4],
                         tern_g=tern_g, tern_u=tern_u, tern_d=tern_d,
                         mw=(mw_g, mw_u, mw_d), tag="f2")
                E.barrier(S["bar"])

            def fin(s):
                s.wait_ge(S["st"].h, S["st"].n)
                s.wait_ge(S["gs"].h, S["gs"].n)
            E.sync(fin)
            E.flush()

    return nc


def emit_attn_a(nc, E, S, cn, x_src, dst, qkv_wT, qkv_b, off_w, lin2_wT, lin2_b):
    identB, identF, C2, U1 = cn["identB"], cn["identF"], cn["C2"], cn["U1"]
    ones_col = cn["ones_col"]
    ones_rf = cn["ones_rf"]

    with ExitStack() as ps:
        sb = lambda name, shape, dt: ps.enter_context(nc.sbuf_tensor(name, shape, dt)).ap()
        xt_in = sb("aa_xt", [128, 2, 1024], F32)
        xln_tm = sb("aa_xlntm", [128, 2, 1024], F32)
        xln_fm = sb("aa_xlnfm", [128, 8, 2048], BF16)
        xrelu8 = sb("aa_xrelu", [128, 8, 2048], BF16)
        dxrow = sb("aa_dxrow", [1, 2048], F32)
        dxbB = sb("aa_dxbB", [128, 2048], F32)
        offw_sb = sb("aa_offw", [128, 2, 1024], BF16)
        lin2_sb = sb("aa_lin2", [128, 2, 1], BF16)
        lin2b_sb = sb("aa_lin2b", [1, 1], F32)
        combo_bf = sb("aa_combo", [128, 8], BF16)
        qkvb_sb = sb("aa_qkvb", [128, 24], F32)
        qkvbq_sb = sb("aa_qkvbq", [128, 24], F32)
        wg_sb = sb("aa_wg", [128, 8, 768], BF16)
        Qfm = sb("aa_Q", [128, 2, 2048], BF16)
        Kfm = sb("aa_K", [128, 2, 2048], BF16)
        Vfm = sb("aa_V", [128, 2, 2048], BF16)
        Vtm = sb("aa_Vtm", [128, 16, 256], BF16)
        Eraw = sb("aa_Eraw", [128, 512], BF16)
        t1f = sb("aa_t1f", [128, 512], F32)
        Em = sb("aa_Em", [128, 512], BF16)
        avf = sb("aa_avf", [128, 2, 512], F32)
        srow = sb("aa_srow", [1, 512], F32)
        rr_tm = sb("aa_rrtm", [128, 4], F32)
        ot_slab = sb("aa_ot", [128, 4, 256], F32)
        x0slab = sb("aa_x0s", [128, 4, 256], F32)
        ln_sc = dict(st=sb("aa_lnst", [128, 2, 6], F32),
                     ag=sb("aa_lnag", [128, 2, 2], F32),
                     tmp=sb("aa_lntmp", [128, 8], F32))

        with (
            nc.psum_tensor("aa_pA", [128, 512], F32) as _pA,
            nc.psum_tensor("aa_pB", [128, 512], F32) as _pB,
            nc.psum_tensor("aa_pC", [128, 512], F32) as _pC,
            nc.psum_tensor("aa_pD", [128, 512], F32) as _pD,
            nc.psum_tensor("aa_pRow", [1, 512], F32) as _pRow,
            nc.psum_tensor("aa_pT", [128, 512], F32) as _pT,
            nc.psum_tensor("aa_pTb", [128, 256], BF16) as _pTb,
            nc.psum_tensor("aa_pT2", [128, 512], F32) as _pT2,
        ):
            pA, pB, pC, pD, pRow, pT, pTb, pT2 = (
                x.ap() for x in (_pA, _pB, _pC, _pD, _pRow, _pT, _pTb, _pT2))

            tgl = S["gl"].mark(4 * 16)

            def c_gp(g):
                g.dma_start(out=offw_sb[:, 0, :], in_=off_w[0:128, :]).then_inc(S["gl"].h, 16)
                g.dma_start(out=offw_sb[:, 1, :], in_=off_w[128:256, :]).then_inc(S["gl"].h, 16)
                g.dma_start(out=lin2_sb[:, 0, :], in_=lin2_wT[0:128, :]).then_inc(S["gl"].h, 16)
                g.dma_start(out=lin2_sb[:, 1, :], in_=lin2_wT[128:256, :]).then_inc(S["gl"].h, 16)
            E.gpsimd(c_gp)

            tld = S["ld"].mark(2 * 16)

            def c_sy(s):
                s.dma_start(out=lin2b_sb, in_=lin2_b[None, :]).then_inc(S["ld"].h, 16)
                s.dma_start(out=qkvb_sb, in_=qkv_b.rearrange("(c p) -> p c", p=128), allow_slow_non_contiguous=True).then_inc(S["ld"].h, 16)
            E.sync(c_sy)

            tve = S["ve"].mark()

            def c_ve(v, tld=tld):
                v.wait_ge(S["ld"].h, tld)
                v.tensor_scalar(qkvbq_sb, qkvb_sb, QSCALE, None, ALU.mult)
                v.drain().then_inc(S["ve"].h, 1)
            E.vector(c_ve)

            tpe = S["pe"].mark()

            def c_pe(t, tgl=tgl):
                t.wait_ge(S["gl"].h, tgl)
                for hc in range(8):
                    for dc in range(2):
                        w = offw_sb[:, dc, hc * 128:(hc + 1) * 128]
                        t.ldweights(w)
                        t.matmul(pT[:, hc:hc + 1], w, lin2_sb[:, dc, :],
                                 start=(dc == 0), stop=(dc == 1))
                t.drain().then_inc(S["pe"].h, 1)
            E.tensor(c_pe)

            tve2 = S["ve"].mark()

            def c_ve2(v, tpe=tpe):
                v.wait_ge(S["pe"].h, tpe)
                v.tensor_copy(combo_bf, pT[:, 0:8])
                v.drain().then_inc(S["ve"].h, 1)
            E.vector(c_ve2)

            for s_i in range(BC if AA_STAGE >= 1 else 0):
                tbase = s_i * L

                for ti in range(NTS):
                    buf = ti % 2
                    t0 = tbase + ti * 128
                    need_ve = S["ve"].n if ti >= 2 else 0
                    tld = S["ld"].mark(16)

                    def ldt(s, t0=t0, buf=buf, need_ve=need_ve, tld=tld):
                        if need_ve:
                            s.wait_ge(S["ve"].h, need_ve)
                        s.dma_start(out=xt_in[:, buf, :],
                                    in_=x_src[t0:t0 + 128, :]).then_inc(S["ld"].h, 16)
                    E.sync(ldt)

                    if AA_STAGE == 11:
                        continue
                    tve_s1 = S["ve"].mark()

                    def ln1(v, buf=buf, tld=tld):
                        v.wait_ge(S["ld"].h, tld)
                        ln_chain(v, None, xt_in[:, buf, :], None, ln_sc, "v1")
                        v.drain().then_inc(S["ve"].h, 1)
                    E.vector(ln1)

                    if AA_STAGE == 12:
                        continue
                    tac_s = S["ac"].mark()

                    def ln2(a, tve_s1=tve_s1):
                        a.wait_ge(S["ve"].h, tve_s1)
                        ln_chain(None, a, None, None, ln_sc, "a")
                        a.drain().then_inc(S["ac"].h, 1)
                    E.scalar(ln2)

                    need_pe = S["pe"].n if ti >= 2 else 0
                    tve_s2 = S["ve"].mark()

                    def ln3(v, buf=buf, tac_s=tac_s, need_pe=need_pe):
                        v.wait_ge(S["ac"].h, tac_s)
                        if need_pe:
                            v.wait_ge(S["pe"].h, need_pe)
                        ln_chain(v, None, xt_in[:, buf, :], xln_tm[:, buf, :],
                                 ln_sc, "v2")
                        v.drain().then_inc(S["ve"].h, 1)
                    E.vector(ln3)

                    if AA_STAGE == 13:
                        continue
                    for hc in range(8):
                        half = hc % 2
                        need_ve2 = S["ve"].n if hc >= 2 else 0
                        tpe = S["pe"].mark()

                        def trs(t, buf=buf, hc=hc, half=half, tve_s2=tve_s2,
                                need_ve2=need_ve2):
                            if hc == 0:
                                t.wait_ge(S["ve"].h, tve_s2)
                            if need_ve2:
                                t.wait_ge(S["ve"].h, need_ve2)
                            dst = pT if half == 0 else pT2
                            t.transpose(dst[:, 0:128],
                                        xln_tm[:, buf, hc * 128:(hc + 1) * 128],
                                        identF)
                            t.drain().then_inc(S["pe"].h, 1)
                        E.tensor(trs)

                        tve = S["ve"].mark()

                        def ev(v, ti=ti, hc=hc, half=half, tpe=tpe):
                            v.wait_ge(S["pe"].h, tpe)
                            src_ps = pT if half == 0 else pT2
                            v.tensor_copy(xln_fm[:, hc, ti * 128:(ti + 1) * 128],
                                          src_ps[:, 0:128])
                            v.drain().then_inc(S["ve"].h, 1)
                        E.vector(ev)

                tve_all = S["ve"].n
                tac = S["ac"].mark(8)

                def relu_all(a, tve_all=tve_all):
                    a.wait_ge(S["ve"].h, tve_all)
                    for hc in range(8):
                        a.activation(xrelu8[:, hc, :], xln_fm[:, hc, :], AFT.Relu)
                    a.drain().then_inc(S["ac"].h, 8)
                E.scalar(relu_all)

                if AA_STAGE < 2:
                    continue
                tve_combo = S["ve"].n
                tpe0_dx = S["pe"].n
                tac0_dx = S["ac"].n
                S["pe"].mark(4)
                S["ac"].mark(4)

                def dx_pe(t, tac=tac, tve_combo=tve_combo, tac0_dx=tac0_dx):
                    t.wait_ge(S["ac"].h, tac)
                    t.wait_ge(S["ve"].h, tve_combo)
                    for q4 in range(4):
                        if q4 > 0:
                            t.wait_ge(S["ac"].h, tac0_dx + q4)
                        for hc in range(8):
                            t.ldweights(combo_bf[:, hc:hc + 1])
                            t.matmul(pRow, combo_bf[:, hc:hc + 1],
                                     xrelu8[:, hc, q4 * 512:(q4 + 1) * 512],
                                     start=(hc == 0), stop=(hc == 7))
                        t.drain().then_inc(S["pe"].h, 1)
                E.tensor(dx_pe)

                def dx_ac(a, tpe0_dx=tpe0_dx):
                    for q4 in range(4):
                        a.wait_ge(S["pe"].h, tpe0_dx + q4 + 1)
                        a.activation(dxrow[:, q4 * 512:(q4 + 1) * 512], pRow,
                                     AFT.Sigmoid, bias=lin2b_sb)
                        a.drain().then_inc(S["ac"].h, 1)
                E.scalar(dx_ac)
                tac_dx = S["ac"].n

                tve_dx = S["ve"].mark()

                def dx_ve(v, tac_dx=tac_dx):
                    v.wait_ge(S["ac"].h, tac_dx)
                    v.tensor_scalar(dxrow, dxrow, float(L), None, ALU.mult)
                    v.drain().then_inc(S["ve"].h, 1)
                E.vector(dx_ve)

                tpe0_bc = S["pe"].n
                tve0_bc = S["ve"].n
                S["pe"].mark(4)
                S["ve"].mark(4)

                def dx_bc_pe(t, tve_dx=tve_dx, tve0_bc=tve0_bc):
                    t.wait_ge(S["ve"].h, tve_dx)
                    for q in range(4):
                        ps = pA if q % 2 == 0 else pB
                        if q >= 2:
                            t.wait_ge(S["ve"].h, tve0_bc + q - 1)
                        t.matmul(ps, ones_rf, dxrow[:, q * 512:(q + 1) * 512],
                                 start=True, stop=True)
                        t.drain().then_inc(S["pe"].h, 1)
                E.tensor(dx_bc_pe)

                def dx_bc_ve(v, tpe0_bc=tpe0_bc):
                    for q in range(4):
                        ps = pA if q % 2 == 0 else pB
                        v.wait_ge(S["pe"].h, tpe0_bc + q + 1)
                        v.tensor_copy(dxbB[:, q * 512:(q + 1) * 512], ps)
                        v.drain().then_inc(S["ve"].h, 1)
                E.vector(dx_bc_ve)
                tgp_dx = S["gp"].n

                for g_i in range(G if AA_STAGE >= 3 else 0):
                    ws = WS[g_i]

                    tgl = S["gl"].mark(3 * 16)
                    tpe_now = S["pe"].n

                    def wg_ld(g, g_i=g_i, tpe_now=tpe_now):
                        g.wait_ge(S["pe"].h, tpe_now)
                        for part in range(3):
                            foff = part * D + g_i * DG
                            src = AP(qkv_wT.tensor, foff,
                                     [[3 * D, 128], [128 * 3 * D, 8], [1, 256]])
                            g.dma_start(out=wg_sb[:, :, part * 256:(part + 1) * 256],
                                        in_=src).then_inc(S["gl"].h, 16)
                    E.gpsimd(wg_ld)

                    qkv_units = []
                    for part in range(3):
                        for dc in range(2):
                            for t4 in range(4):
                                u = len(qkv_units)
                                psum = pA if (u % 2 == 0) else pB
                                need_ve = qkv_units[u - 2][1] if u >= 2 else 0
                                tpe_u = S["pe"].mark()

                                def qk_pe(t, part=part, dc=dc, t4=t4, psum=psum,
                                          need_ve=need_ve, tgl=tgl, u=u,
                                          tve_bc=S["ve"].n):
                                    if u == 0:
                                        t.wait_ge(S["gl"].h, tgl)
                                        t.wait_ge(S["ve"].h, tve_bc)
                                    if need_ve:
                                        t.wait_ge(S["ve"].h, need_ve)
                                    for hc in range(8):
                                        w = wg_sb[:, hc, part * 256 + dc * 128:
                                                  part * 256 + (dc + 1) * 128]
                                        t.ldweights(w)
                                        t.matmul(psum, w,
                                                 xln_fm[:, hc, t4 * 512:(t4 + 1) * 512],
                                                 start=(hc == 0), stop=(hc == 7))
                                    t.drain().then_inc(S["pe"].h, 1)
                                E.tensor(qk_pe)

                                tve_u = S["ve"].mark()

                                def qk_ve(v, part=part, dc=dc, t4=t4, psum=psum,
                                          g_i=g_i, tpe_u=tpe_u):
                                    v.wait_ge(S["pe"].h, tpe_u)
                                    dest = (Qfm, Kfm, Vfm)[part]
                                    col = part * 8 + g_i * 2 + dc
                                    bias = (qkvbq_sb if part == 0 else qkvb_sb)[:, col:col + 1]
                                    sc = QSCALE if part == 0 else 1.0
                                    v.tensor_scalar(dest[:, dc, t4 * 512:(t4 + 1) * 512],
                                                    psum, sc, bias, ALU.mult, ALU.add)
                                    v.drain().then_inc(S["ve"].h, 1)
                                E.vector(qk_ve)
                                qkv_units.append((tpe_u, tve_u))

                    vt_units = []
                    for jc in range(16):
                        need_ve = vt_units[jc - 1][1] if jc >= 1 else 0
                        tpe_v = S["pe"].mark()

                        def vt_pe(t, jc=jc, need_ve=need_ve,
                                  last_qkv=qkv_units[-1][1]):
                            if jc == 0:
                                t.wait_ge(S["ve"].h, last_qkv)
                            if need_ve:
                                t.wait_ge(S["ve"].h, need_ve)
                            for dc in range(2):
                                t.transpose(pTb[:, dc * 128:(dc + 1) * 128],
                                            Vfm[:, dc, jc * 128:(jc + 1) * 128],
                                            identB)
                            t.drain().then_inc(S["pe"].h, 1)
                        E.tensor(vt_pe)

                        tve_v = S["ve"].mark()

                        def vt_ve(v, jc=jc, tpe_v=tpe_v):
                            v.wait_ge(S["pe"].h, tpe_v)
                            v.tensor_copy(Vtm[:, jc, :], pTb[:, 0:256])
                            v.drain().then_inc(S["ve"].h, 1)
                        E.vector(vt_ve)
                        vt_units.append((tpe_v, tve_v))

                    for it in range(4 if AA_STAGE >= 4 else 0):
                        i0 = it * 512
                        jcs = list(range(4 * it, 16))
                        n_jc = len(jcs)

                        tld_x = S["ld"].mark(16)
                        tve2_now = S["ve2"].n

                        def xs_ld(s, it=it, g_i=g_i, tbase=tbase,
                                  tve2_now=tve2_now):
                            s.wait_ge(S["ve2"].h, tve2_now)
                            src = AP(x_src.tensor,
                                     (tbase + it * 512) * D + g_i * DG,
                                     [[D, 128], [128 * D, 4], [1, 256]])
                            s.dma_start(out=x0slab, in_=src).then_inc(S["ld"].h, 16)
                        E.sync(xs_ld)

                        prev_tac = None
                        prev_tve = None
                        for idx, jc in enumerate(jcs):
                            u0 = i0 + ws + 2048 - jc * 128
                            diag = (jc * 128 < i0 + 512) and (jc * 128 + 127 >= i0)
                            last = (idx == n_jc - 1)

                            tve_before = S["ve"].n
                            tpe_s = S["pe"].mark()

                            def sc_pe(t, jc=jc, i0=i0, idx=idx,
                                      prev_tac=prev_tac, tve_before=tve_before):
                                if idx == 0:
                                    t.wait_ge(S["ve"].h, tve_before)
                                else:
                                    t.wait_ge(S["ac"].h, prev_tac)
                                for dc in range(2):
                                    k = Kfm[:, dc, jc * 128:(jc + 1) * 128]
                                    t.ldweights(k)
                                    t.matmul(pA, k, Qfm[:, dc, i0:i0 + 512],
                                             start=(dc == 0), stop=(dc == 1))
                                t.drain().then_inc(S["pe"].h, 1)
                            E.tensor(sc_pe)

                            tac_s = S["ac"].mark()

                            def sc_ac(a, tpe_s=tpe_s, prev_tve=prev_tve):
                                a.wait_ge(S["pe"].h, tpe_s)
                                if prev_tve is not None:
                                    a.wait_ge(S["ve"].h, prev_tve)
                                a.activation(Eraw, pA, AFT.Exp)
                                a.drain().then_inc(S["ac"].h, 1)
                            E.scalar(sc_ac)

                            tve_s = S["ve"].mark()

                            def sc_ve(v, u0=u0, i0=i0, diag=diag, ws=ws,
                                      tac_s=tac_s, prev_pe_av=S["pe"].n):
                                v.wait_ge(S["ac"].h, tac_s)
                                v.wait_ge(S["pe"].h, prev_pe_av)
                                v.tensor_tensor(t1f, dxbB[:, i0:i0 + 512],
                                                C2[:, u0:u0 + 512], ALU.subtract)
                                v.scalar_tensor_tensor(Em, t1f, 0.0, Eraw,
                                                       ALU.is_ge, ALU.mult)
                                if diag:
                                    v.tensor_tensor(Em, Em,
                                                    U1[:, u0 - ws:u0 - ws + 512],
                                                    ALU.mult)
                                v.drain().then_inc(S["ve"].h, 1)
                            E.vector(sc_ve)

                            if last:
                                tpe_av = S["pe2"].mark()
                            tpe_avu = S["pe"].mark()

                            def av_pe(t, jc=jc, idx=idx, last=last, tve_s=tve_s):
                                t.wait_ge(S["ve"].h, tve_s)
                                t.ldweights(Vtm[:, jc, 0:128])
                                t.matmul(pC, Vtm[:, jc, 0:128], Em,
                                         start=(idx == 0), stop=last)
                                t.ldweights(Vtm[:, jc, 128:256])
                                t.matmul(pD, Vtm[:, jc, 128:256], Em,
                                         start=(idx == 0), stop=last)
                                t.ldweights(ones_col)
                                t.matmul(pRow, ones_col, Em,
                                         start=(idx == 0), stop=last)
                                t.drain().then_inc(S["pe"].h, 1)
                                if last:
                                    t.nop().then_inc(S["pe2"].h, 1)
                            E.tensor(av_pe)
                            prev_tac = tac_s
                            prev_tve = tve_s

                        tve_e1 = S["ve"].mark()

                        def ep_ve(v, tpe_av=tpe_av):
                            v.wait_ge(S["pe2"].h, tpe_av)
                            v.tensor_copy(avf[:, 0, :], pC)
                            v.tensor_copy(avf[:, 1, :], pD)
                            v.tensor_copy(srow, pRow)
                            v.drain()
                            v.reciprocal(srow, srow)
                            v.drain().then_inc(S["ve"].h, 1)
                        E.vector(ep_ve)

                        tpe_e1 = S["pe"].mark()

                        def ep_pe(t, tve_e1=tve_e1):
                            t.wait_ge(S["ve"].h, tve_e1)
                            for q in range(4):
                                t.transpose(pT[:, q * 128:q * 128 + 1],
                                            srow[:, q * 128:(q + 1) * 128],
                                            identF[0:1, 0:1])
                            t.drain().then_inc(S["pe"].h, 1)
                        E.tensor(ep_pe)

                        tve_e2 = S["ve"].mark()

                        def ep_ve2(v, tpe_e1=tpe_e1):
                            v.wait_ge(S["pe"].h, tpe_e1)
                            for q in range(4):
                                v.tensor_copy(rr_tm[:, q:q + 1],
                                              pT[:, q * 128:q * 128 + 1])
                            v.drain().then_inc(S["ve"].h, 1)
                        E.vector(ep_ve2)

                        for q in range(4):
                            tpe_q = S["pe"].mark()

                            def ep_pe2(t, q=q, tve_e2=tve_e2,
                                       need_ve=(S["ve2"].n if q >= 1 else 0)):
                                if q == 0:
                                    t.wait_ge(S["ve"].h, tve_e2)
                                if need_ve:
                                    t.wait_ge(S["ve2"].h, need_ve)
                                for dc in range(2):
                                    t.transpose(pT[:, dc * 128:(dc + 1) * 128],
                                                avf[:, dc, q * 128:(q + 1) * 128],
                                                identF)
                                t.drain().then_inc(S["pe"].h, 1)
                            E.tensor(ep_pe2)

                            tve_q = S["ve2"].mark()

                            def ep_ve3(v, q=q, tpe_q=tpe_q, tld_x=tld_x):
                                v.wait_ge(S["pe"].h, tpe_q)
                                if q == 0:
                                    v.wait_ge(S["ld"].h, tld_x)
                                v.tensor_scalar(ot_slab[:, q, :], pT[:, 0:256],
                                                rr_tm[:, q:q + 1], None, ALU.mult)
                                v.drain()
                                v.tensor_tensor(ot_slab[:, q, :], ot_slab[:, q, :],
                                                x0slab[:, q, :], ALU.add)
                                v.drain().then_inc(S["ve2"].h, 1)
                            E.vector(ep_ve3)

                        tst = S["st"].mark(16)

                        def ep_st(s, it=it, g_i=g_i, tbase=tbase,
                                  tve_q=S["ve2"].n):
                            s.wait_ge(S["ve2"].h, tve_q)
                            dsta = AP(dst.tensor,
                                      (tbase + it * 512) * D + g_i * DG,
                                      [[D, 128], [128 * D, 4], [1, 256]])
                            s.dma_start(out=dsta, in_=ot_slab).then_inc(S["st"].h, 16)
                        E.sync(ep_st)

                        tst_done = S["st"].n

                        def ep_wait(v, tst_done=tst_done):
                            v.wait_ge(S["st"].h, tst_done)
                        E.vector(ep_wait)

            E.flush()


def emit_wquant(nc, E, S, cn, w_list, mw_out):
    ones_rf = cn["ones_rf"]
    identF = cn["identF"]

    with ExitStack() as ps:
        sb = lambda name, shape, dt: ps.enter_context(nc.sbuf_tensor(name, shape, dt)).ap()
        wt = sb("wq_wt", [128, 2, 4096], F32)
        acc = sb("wq_acc", [128, 1], F32)
        rcol = sb("wq_rcol", [128, 1], F32)
        prow = sb("wq_prow", [1, 128], F32)
        sc1 = sb("wq_sc1", [1, 1], F32)
        zi32 = sb("wq_zi", [128, 4096], I32)
        c2f = sb("wq_c2f", [128, 4096], F16)
        bq = sb("wq_bq", [128, 4096], BF16)
        tq = sb("wq_tq", [128, 4096], BF16)

        with (
            nc.psum_tensor("wq_p0", [128, 128], F32) as _p0,
            nc.psum_tensor("wq_p1", [128, 1], F32) as _p1,
        ):
            p0, p1 = _p0.ap(), _p1.ap()

            for wi, (w_hbm, t_hbm, rows) in enumerate(w_list):
                cols = (INNER * D) // rows
                ntile = rows // 128
                smw = mw_out[wi]

                tve0 = S["ve"].mark()

                def zacc(v, tve0=tve0, tpe_now=S["pe"].n):
                    v.wait_ge(S["pe"].h, tpe_now)
                    v.memset(acc, 0.0)
                    v.drain().then_inc(S["ve"].h, 1)
                E.vector(zacc)

                for ti in range(ntile):
                    buf = ti % 2
                    need_ve = S["ve"].n
                    tld = S["ld"].mark(16)

                    def wld(s, ti=ti, buf=buf, cols=cols, w_hbm=w_hbm,
                            need_ve=need_ve, tld=tld):
                        if need_ve:
                            s.wait_ge(S["ve"].h, need_ve)
                        s.dma_start(out=wt[:, buf, 0:cols],
                                    in_=w_hbm[ti * 128:(ti + 1) * 128, :]).then_inc(S["ld"].h, 16)
                    E.sync(wld)

                    tve = S["ve"].mark()

                    def wred(v, buf=buf, cols=cols, tld=tld):
                        v.wait_ge(S["ld"].h, tld)
                        v.tensor_reduce(rcol, wt[:, buf, 0:cols], AX.X, ALU.add,
                                        apply_absolute_value=True)
                        v.drain()
                        v.tensor_add(acc, acc, rcol)
                        v.drain().then_inc(S["ve"].h, 1)
                    E.vector(wred)

                tpe = S["pe"].mark()

                def wtr(t, tve_last=S["ve"].n):
                    t.wait_ge(S["ve"].h, tve_last)
                    t.transpose(p0[:, 0:1].rearrange("p a -> p a") if False else p0[0:1, 0:128],
                                acc, identF)
                    t.drain().then_inc(S["pe"].h, 1)
                E.tensor(wtr)

                tve_m = S["ve"].mark()

                def wm(v, tpe=tpe, rows=rows, cols=cols):
                    v.wait_ge(S["pe"].h, tpe)
                    v.tensor_copy(prow, p0[0:1, 0:128])
                    v.drain()
                    v.tensor_reduce(sc1, prow, AX.X, ALU.add)
                    v.drain()
                    v.tensor_scalar(sc1, sc1, 1.0 / (rows * cols), None, ALU.mult)
                    v.drain()
                    v.tensor_scalar(sc1, sc1, 1e-5, None, ALU.max)
                    v.drain().then_inc(S["ve"].h, 1)
                E.vector(wm)

                tpe_b = S["pe"].mark()

                def wbc(t, tve_m=tve_m):
                    t.wait_ge(S["ve"].h, tve_m)
                    t.matmul(p1, ones_rf, sc1, start=True, stop=True)
                    t.drain().then_inc(S["pe"].h, 1)
                E.tensor(wbc)

                tve_b = S["ve"].mark()

                def wev(v, tpe_b=tpe_b, smw=smw):
                    v.wait_ge(S["pe"].h, tpe_b)
                    v.tensor_copy(smw, p1)
                    v.drain()
                    v.reciprocal(rcol, smw)
                    v.drain().then_inc(S["ve"].h, 1)
                E.vector(wev)

                for ti in range(ntile):
                    buf = ti % 2
                    need_ve = S["ve"].n
                    tld = S["ld"].mark(16)

                    def wld2(s, ti=ti, buf=buf, cols=cols, w_hbm=w_hbm,
                             need_ve=need_ve, tld=tld):
                        if need_ve:
                            s.wait_ge(S["ve"].h, need_ve)
                        s.dma_start(out=wt[:, buf, 0:cols],
                                    in_=w_hbm[ti * 128:(ti + 1) * 128, :]).then_inc(S["ld"].h, 16)
                    E.sync(wld2)

                    tgs_prev = S["gs"].n
                    tve_q = S["ve"].mark()

                    def wquant(v, buf=buf, cols=cols, tld=tld, tgs_prev=tgs_prev):
                        v.wait_ge(S["ld"].h, tld)
                        v.wait_ge(S["gs"].h, tgs_prev)
                        w_ap = wt[:, buf, 0:cols]
                        v.tensor_scalar(zi32[:, 0:cols], w_ap, rcol, 256.0,
                                        ALU.mult, ALU.add)
                        v.tensor_scalar(c2f[:, 0:cols], zi32[:, 0:cols], -255.5,
                                        None, ALU.add)
                        v.scalar_tensor_tensor(bq[:, 0:cols], w_ap, rcol,
                                               c2f[:, 0:cols], ALU.mult, ALU.is_ge)
                        v.scalar_tensor_tensor(tq[:, 0:cols], zi32[:, 0:cols],
                                               -256.0, bq[:, 0:cols], ALU.add, ALU.add)
                        v.drain()
                        v.tensor_scalar(tq[:, 0:cols], tq[:, 0:cols], 1.0, -1.0,
                                        ALU.min, ALU.max)
                        v.drain().then_inc(S["ve"].h, 1)
                    E.vector(wquant)

                    tgs = S["gs"].mark(16)

                    def wst(g, ti=ti, cols=cols, t_hbm=t_hbm, tve_q=tve_q):
                        g.wait_ge(S["ve"].h, tve_q)
                        g.dma_start(out=t_hbm[ti * 128:(ti + 1) * 128, :],
                                    in_=tq[:, 0:cols]).then_inc(S["gs"].h, 16)
                    E.gpsimd(wst)

            E.flush()


def emit_ffn(nc, E, S, cn, src, dst, tern_g, tern_u, tern_d, mw, tag):
    identB, identF, identH = cn["identB"], cn["identF"], cn["identH"]
    mw_g, mw_u, mw_d = mw
    TT = 512
    NSUB = TT // 128
    NTT = T // TT

    with ExitStack() as ps:
        sb = lambda name, shape, dt: ps.enter_context(nc.sbuf_tensor(tag + name, shape, dt)).ap()
        rtile = sb("_rt", [128, NSUB, 1024], F32)
        xlns = sb("_xln", [128, 1024], F32)
        xq_tm = sb("_xqtm", [128, NSUB, 1024], F16)
        xq_fm = sb("_xqfm", [128, 8, TT], BF16)
        h_tm = sb("_h", [128, NSUB, 4096], F16)
        hq_fm = sb("_hqfm", [128, 32, TT], BF16)
        wgch = sb("_wg", [128, 2, 8, 512], BF16)
        wuch = sb("_wu", [128, 2, 8, 512], BF16)
        wdch = sb("_wd", [128, 2, 32, 128], BF16)
        sgb = sb("_sg", [128, 512], F32)
        zi32 = sb("_zi", [128, 1024], I32)
        c2f = sb("_c2f", [128, 1024], F16)
        bq = sb("_bq", [128, 1024], BF16)
        dfs = sb("_dfs", [128, 512], F32)
        am2 = sb("_am2", [128, NSUB], F32)
        svec = sb("_svec", [128, NSUB], F32)
        srg = sb("_srg", [128, NSUB], F32)
        sru = sb("_sru", [128, NSUB], F32)
        cvec = sb("_cv", [128, NSUB], F32)
        s3v = sb("_s3", [128, NSUB], F32)
        rmax = sb("_rmax", [128, NSUB], F32)
        tmp1 = sb("_tmp1", [128, 8], F32)
        ln_sc = dict(st=sb("_lnst", [128, 2, 6], F32),
                     ag=sb("_lnag", [128, 2, 2], F32),
                     tmp=sb("_lntmp", [128, 8], F32))

        with (
            nc.psum_tensor(tag + "pg0", [128, 512], F32) as _pg0,
            nc.psum_tensor(tag + "pg1", [128, 512], F32) as _pg1,
            nc.psum_tensor(tag + "pu0", [128, 512], F32) as _pu0,
            nc.psum_tensor(tag + "pu1", [128, 512], F32) as _pu1,
            nc.psum_tensor(tag + "pd", [128, 512], F32) as _pd,
            nc.psum_tensor(tag + "pb0", [128, 128], F16) as _pb0,
            nc.psum_tensor(tag + "pb1", [128, 128], F16) as _pb1,
        ):
            pg = [_pg0.ap(), _pg1.ap()]
            pu = [_pu0.ap(), _pu1.ap()]
            pd = _pd.ap()
            pb = [_pb0.ap(), _pb1.ap()]

            for tt in range(NTT):
                t0 = tt * TT

                tst_prev = S["st"].n
                tld_r = S["ld"].mark(16)

                def r_ld(s, t0=t0, tst_prev=tst_prev, tld_r=tld_r):
                    s.wait_ge(S["st"].h, tst_prev)
                    src_ap = AP(src.tensor, t0 * D,
                                [[D, 128], [128 * D, NSUB], [1, D]])
                    s.dma_start(out=rtile, in_=src_ap).then_inc(S["ld"].h, 16)
                E.sync(r_ld)

                for sub in range(NSUB):
                    tve_s1 = S["ve"].mark()

                    def f_ln1(v, sub=sub, tld_r=tld_r):
                        if sub == 0:
                            v.wait_ge(S["ld"].h, tld_r)
                        ln_chain(v, None, rtile[:, sub, :], None, ln_sc, "v1")
                        v.drain().then_inc(S["ve"].h, 1)
                    E.vector(f_ln1)

                    tac_s = S["ac"].mark()

                    def f_ln2(a, tve_s1=tve_s1):
                        a.wait_ge(S["ve"].h, tve_s1)
                        ln_chain(None, a, None, None, ln_sc, "a")
                        a.drain().then_inc(S["ac"].h, 1)
                    E.scalar(f_ln2)

                    tve_s2 = S["ve"].mark()

                    def f_ln3(v, sub=sub, tac_s=tac_s):
                        v.wait_ge(S["ac"].h, tac_s)
                        ln_chain(v, None, rtile[:, sub, :], xlns, ln_sc, "v2")
                        v.drain()
                        v.tensor_reduce(tmp1[:, 0:1], xlns, AX.X, ALU.max,
                                        apply_absolute_value=True)
                        v.drain()
                        v.tensor_scalar(am2[:, sub:sub + 1], tmp1[:, 0:1], 1e-5,
                                        None, ALU.max)
                        v.drain()
                        v.reciprocal(tmp1[:, 1:2], am2[:, sub:sub + 1])
                        v.drain()
                        v.tensor_scalar(svec[:, sub:sub + 1], tmp1[:, 1:2], 127.0,
                                        None, ALU.mult)
                        v.tensor_scalar(srg[:, sub:sub + 1], am2[:, sub:sub + 1],
                                        mw_g, 1.0 / 127.0, ALU.mult, ALU.mult)
                        v.tensor_scalar(sru[:, sub:sub + 1], am2[:, sub:sub + 1],
                                        mw_u, 1.0 / 127.0, ALU.mult, ALU.mult)
                        v.drain()
                        v.tensor_scalar(zi32, xlns, svec[:, sub:sub + 1], 256.0,
                                        ALU.mult, ALU.add)
                        v.tensor_scalar(c2f, zi32, -255.5, None, ALU.add)
                        v.scalar_tensor_tensor(bq, xlns, svec[:, sub:sub + 1],
                                               c2f, ALU.mult, ALU.is_ge)
                        v.scalar_tensor_tensor(xq_tm[:, sub, :], zi32, -256.0,
                                               bq, ALU.add, ALU.add)
                        v.drain().then_inc(S["ve"].h, 1)
                    E.vector(f_ln3)

                tq_units = []
                for blk in range(NSUB * 8):
                    sub, hc = blk // 8, blk % 8
                    par = blk % 2
                    need_ve = tq_units[blk - 2][1] if blk >= 2 else 0
                    tpe_b = S["pe"].mark()

                    def xq_tr(t, sub=sub, hc=hc, par=par, need_ve=need_ve,
                              tve_all=S["ve"].n):
                        if blk_first := (sub == 0 and hc == 0):
                            t.wait_ge(S["ve"].h, tve_all)
                        if need_ve:
                            t.wait_ge(S["ve"].h, need_ve)
                        t.transpose(pb[par][:, 0:128],
                                    xq_tm[:, sub, hc * 128:(hc + 1) * 128], identH)
                        t.drain().then_inc(S["pe"].h, 1)
                    E.tensor(xq_tr)

                    tve_b = S["ve"].mark()

                    def xq_ev(v, sub=sub, hc=hc, par=par, tpe_b=tpe_b):
                        v.wait_ge(S["pe"].h, tpe_b)
                        v.tensor_copy(xq_fm[:, hc, sub * 128:(sub + 1) * 128],
                                      pb[par][:, 0:128])
                        v.drain().then_inc(S["ve"].h, 1)
                    E.vector(xq_ev)
                    tq_units.append((tpe_b, tve_b))

                gu_ep = []
                for u in range(8 * NSUB):
                    fch, tc = divmod(u, NSUB)
                    wbuf = fch % 2
                    par = u % 2
                    if tc == 0:
                        need_pe = gu_ep[(fch - 2) * NSUB + NSUB - 1][2] if fch >= 2 else 0
                        twl = S["wl"].mark(32)

                        def w_ld(s, fch=fch, wbuf=wbuf, need_pe=need_pe, twl=twl):
                            if need_pe:
                                s.wait_ge(S["pe"].h, need_pe)
                            srcg = AP(tern_g.tensor, fch * 512,
                                      [[INNER, 128], [128 * INNER, 8], [1, 512]])
                            srcu = AP(tern_u.tensor, fch * 512,
                                      [[INNER, 128], [128 * INNER, 8], [1, 512]])
                            s.dma_start(out=wgch[:, wbuf, :, :], in_=srcg).then_inc(S["wl"].h, 16)
                            s.dma_start(out=wuch[:, wbuf, :, :], in_=srcu).then_inc(S["wl"].h, 16)
                        E.scalar(w_ld)

                    prev_ep = gu_ep[u - 2] if u >= 2 else None
                    tpe_gu = S["pe"].mark()

                    def gu_pe(t, fch=fch, tc=tc, wbuf=wbuf, par=par, twl=twl,
                              prev_ep=prev_ep, tve_xq=tq_units[-1][1]):
                        if tc == 0:
                            t.wait_ge(S["wl"].h, twl)
                        if fch == 0 and tc == 0:
                            t.wait_ge(S["ve"].h, tve_xq)
                        if prev_ep is not None:
                            t.wait_ge(S["ac"].h, prev_ep[0])
                            t.wait_ge(S["ve"].h, prev_ep[1])
                        for hc in range(8):
                            w = xq_fm[:, hc, tc * 128:(tc + 1) * 128]
                            t.ldweights(w)
                            t.matmul(pg[par], w, wgch[:, wbuf, hc, :],
                                     start=(hc == 0), stop=(hc == 7))
                            t.matmul(pu[par], w, wuch[:, wbuf, hc, :],
                                     start=(hc == 0), stop=(hc == 7))
                        t.drain().then_inc(S["pe"].h, 1)
                    E.tensor(gu_pe)

                    tac_e = S["ac"].mark()

                    def gu_ac(a, tc=tc, par=par, tpe_gu=tpe_gu):
                        a.wait_ge(S["pe"].h, tpe_gu)
                        a.activation(sgb, pg[par], AFT.Silu,
                                     scale=srg[:, tc:tc + 1])
                        a.drain().then_inc(S["ac"].h, 1)
                    E.scalar(gu_ac)

                    tve_e = S["ve"].mark()

                    def gu_ve(v, fch=fch, tc=tc, par=par, tac_e=tac_e):
                        v.wait_ge(S["ac"].h, tac_e)
                        v.scalar_tensor_tensor(h_tm[:, tc, fch * 512:(fch + 1) * 512],
                                               pu[par], 0.00390625, sgb,
                                               ALU.mult, ALU.mult)
                        v.drain()
                        v.tensor_reduce(tmp1[:, 2:3],
                                        h_tm[:, tc, fch * 512:(fch + 1) * 512],
                                        AX.X, ALU.max, apply_absolute_value=True)
                        v.drain()
                        if fch == 0:
                            v.tensor_copy(rmax[:, tc:tc + 1], tmp1[:, 2:3])
                        else:
                            v.tensor_max(rmax[:, tc:tc + 1], rmax[:, tc:tc + 1],
                                         tmp1[:, 2:3])
                        v.drain().then_inc(S["ve"].h, 1)
                    E.vector(gu_ve)
                    gu_ep.append((tac_e, tve_e, tpe_gu))

                for tc in range(NSUB):
                    tve_q = S["ve"].mark()

                    def h_q(v, tc=tc, tve_gu=gu_ep[-1][1]):
                        if tc == 0:
                            v.wait_ge(S["ve"].h, tve_gu)
                        v.tensor_scalar(tmp1[:, 3:4], rmax[:, tc:tc + 1],
                                        sru[:, tc:tc + 1], 256.0, ALU.mult,
                                        ALU.mult)
                        v.drain()
                        v.tensor_scalar(tmp1[:, 3:4], tmp1[:, 3:4], 1e-5, None,
                                        ALU.max)
                        v.drain()
                        v.reciprocal(tmp1[:, 4:5], tmp1[:, 3:4])
                        v.drain()
                        v.tensor_scalar(cvec[:, tc:tc + 1], tmp1[:, 4:5],
                                        sru[:, tc:tc + 1], 127.0 * 256.0,
                                        ALU.mult, ALU.mult)
                        v.tensor_scalar(s3v[:, tc:tc + 1], tmp1[:, 3:4],
                                        mw_d, 1.0 / 127.0, ALU.mult, ALU.mult)
                        v.drain()
                        for q4 in range(4):
                            h_sl = h_tm[:, tc, q4 * 1024:(q4 + 1) * 1024]
                            v.tensor_scalar(zi32, h_sl, cvec[:, tc:tc + 1],
                                            256.0, ALU.mult, ALU.add)
                            v.tensor_scalar(c2f, zi32, -255.5, None, ALU.add)
                            v.scalar_tensor_tensor(bq, h_sl, cvec[:, tc:tc + 1],
                                                   c2f, ALU.mult, ALU.is_ge)
                            v.scalar_tensor_tensor(h_sl, zi32, -256.0, bq,
                                                   ALU.add, ALU.add)
                            v.drain()
                        v.drain().then_inc(S["ve"].h, 1)
                    E.vector(h_q)

                hq_units = []
                for blk in range(NSUB * 32):
                    tc, fc = divmod(blk, 32)
                    par = blk % 2
                    need_ve = hq_units[blk - 2][1] if blk >= 2 else 0
                    tpe_b = S["pe"].mark()

                    def hq_tr(t, tc=tc, fc=fc, par=par, need_ve=need_ve,
                              tve_hq=S["ve"].n, blk=blk):
                        if blk == 0:
                            t.wait_ge(S["ve"].h, tve_hq)
                        if need_ve:
                            t.wait_ge(S["ve"].h, need_ve)
                        t.transpose(pb[par][:, 0:128],
                                    h_tm[:, tc, fc * 128:(fc + 1) * 128], identH)
                        t.drain().then_inc(S["pe"].h, 1)
                    E.tensor(hq_tr)

                    tve_b = S["ve"].mark()

                    def hq_ev(v, tc=tc, fc=fc, par=par, tpe_b=tpe_b):
                        v.wait_ge(S["pe"].h, tpe_b)
                        v.tensor_copy(hq_fm[:, fc, tc * 128:(tc + 1) * 128],
                                      pb[par][:, 0:128])
                        v.drain().then_inc(S["ve"].h, 1)
                    E.vector(hq_ev)
                    hq_units.append((tpe_b, tve_b))

                for hc2 in range(8):
                    wbuf = hc2 % 2
                    need_pe = down_pe_prev if hc2 >= 2 else 0
                    twl = S["wl"].mark(16)

                    def wd_ld(s, hc2=hc2, wbuf=wbuf, twl=twl,
                              need_pe=(S["pe"].n if hc2 >= 2 else 0)):
                        if need_pe:
                            s.wait_ge(S["pe"].h, need_pe)
                        srcd = AP(tern_d.tensor, hc2 * 128,
                                  [[D, 128], [128 * D, 32], [1, 128]])
                        s.dma_start(out=wdch[:, wbuf, :, :], in_=srcd).then_inc(S["wl"].h, 16)
                    E.scalar(wd_ld)

                    prev_ve = S["ve"].n
                    tpe_d = S["pe"].mark()

                    def d_pe(t, hc2=hc2, wbuf=wbuf, twl=twl, prev_ve=prev_ve,
                             tve_hq=hq_units[-1][1]):
                        t.wait_ge(S["wl"].h, twl)
                        if hc2 == 0:
                            t.wait_ge(S["ve"].h, tve_hq)
                        else:
                            t.wait_ge(S["ve"].h, prev_ve)
                        for fc in range(32):
                            w = wdch[:, wbuf, fc, :]
                            t.ldweights(w)
                            t.matmul(pd, w, hq_fm[:, fc, :],
                                     start=(fc == 0), stop=(fc == 31))
                        t.drain().then_inc(S["pe"].h, 1)
                    E.tensor(d_pe)
                    down_pe_prev = tpe_d

                    tve_d = S["ve"].mark()

                    def d_ev(v, hc2=hc2, tpe_d=tpe_d):
                        v.wait_ge(S["pe"].h, tpe_d)
                        v.tensor_copy(dfs, pd)
                        v.drain().then_inc(S["ve"].h, 1)
                    E.vector(d_ev)

                    for q in range(4):
                        par = q % 2
                        need_ve2 = S["ve2"].n if q >= 2 else 0
                        tpe_q = S["pe"].mark()

                        def d_tr(t, q=q, par=par, tve_d=tve_d, need_ve2=need_ve2):
                            if q == 0:
                                t.wait_ge(S["ve"].h, tve_d)
                            if need_ve2:
                                t.wait_ge(S["ve2"].h, need_ve2)
                            dst_ps = pg[par]
                            t.transpose(dst_ps[:, 0:128],
                                        dfs[:, q * 128:(q + 1) * 128], identF)
                            t.drain().then_inc(S["pe"].h, 1)
                        E.tensor(d_tr)

                        tve_q2 = S["ve2"].mark()

                        def d_add(v, q=q, par=par, hc2=hc2, tpe_q=tpe_q):
                            v.wait_ge(S["pe"].h, tpe_q)
                            dst_ps = pg[par]
                            v.scalar_tensor_tensor(
                                rtile[:, q, hc2 * 128:(hc2 + 1) * 128],
                                dst_ps[:, 0:128], s3v[:, q:q + 1],
                                rtile[:, q, hc2 * 128:(hc2 + 1) * 128],
                                ALU.mult, ALU.add)
                            v.drain().then_inc(S["ve2"].h, 1)
                        E.vector(d_add)

                tst = S["st"].mark(16)

                def r_st(s, t0=t0, tve2_d=S["ve2"].n, tst=tst):
                    s.wait_ge(S["ve2"].h, tve2_d)
                    dst_ap = AP(dst.tensor, t0 * D,
                                [[D, 128], [128 * D, NSUB], [1, D]])
                    s.dma_start(out=dst_ap, in_=rtile).then_inc(S["st"].h, 16)
                E.sync(r_st)


            E.flush()

            E.flush()


def emit_attn_b(nc, E, S, cn, src, dst, phazor, phazor_init, last_re, last_im,
                powers_hbm, H_hbm, S_hbm):
    identB, identF, Janti = cn["identB"], cn["identF"], cn["Janti"]
    ones_col, ones_row = cn["ones_col"], cn["ones_row"]

    with ExitStack() as ps:
        sb = lambda name, shape, dt: ps.enter_context(nc.sbuf_tensor(name, shape, dt)).ap()
        x3 = sb("ab_x3", [128, 32, 1024], BF16)
        Pre = sb("ab_Pre", [128, 8, 130], F32)
        Pim = sb("ab_Pim", [128, 8, 130], F32)
        Are = sb("ab_Are", [128, 8, 129], F32)
        Aim = sb("ab_Aim", [128, 8, 129], F32)
        Aext = sb("ab_Aext", [128, 8, 256], F32)
        phin = sb("ab_phin", [128, 8, 2], F32)
        phii = sb("ab_phii", [128, 8, 2], F32)
        mg = sb("ab_mg", [128, 8, 6], F32)
        PimF = sb("ab_PimF", [128, 1024], BF16)
        phpre = sb("ab_phpre", [128, 1024], BF16)
        phpim = sb("ab_phpim", [128, 1024], BF16)
        Tb = sb("ab_Tb", [128, 2, 32, 128], BF16)
        Hre = sb("ab_Hre", [128, 8, 32], F32)
        Him = sb("ab_Him", [128, 8, 32], F32)
        Sre = sb("ab_Sre", [128, 8, 33], F32)
        Sim = sb("ab_Sim", [128, 8, 33], F32)
        lastv = sb("ab_last", [128, 8, 2], F32)
        hstg = sb("ab_hstg", [1, 1024], F32)
        Hstage = sb("ab_Hstage", [32, 2, 1024], F32)
        St = sb("ab_St", [33, 2, 8, 128], F32)
        srow = sb("ab_srow", [1, 2, 2, 1024], BF16)
        tmA = sb("ab_tmA", [128, 1040], F32)
        tmB = sb("ab_tmB", [128, 1040], F32)
        tmC = sb("ab_tmC", [128, 1024], BF16)
        t128 = sb("ab_t128", [128, 128], BF16)
        rt2 = sb("ab_rt2", [128, 2, 1024], F32)
        xt2 = sb("ab_xt2", [128, 2, 1024], F32)
        ln_sc = dict(st=sb("ab_lnst", [128, 2, 6], F32),
                     ag=sb("ab_lnag", [128, 2, 2], F32),
                     tmp=sb("ab_lntmp", [128, 8], F32))

        with (
            nc.psum_tensor("ab_pc0", [128, 32], F32) as _pc0,
            nc.psum_tensor("ab_pc1", [128, 32], F32) as _pc1,
            nc.psum_tensor("ab_pbr", [128, 1024], F32) as _pbr,
            nc.psum_tensor("ab_pbi", [128, 1024], F32) as _pbi,
        ):
            pc = [_pc0.ap(), _pc1.ap()]
            pbr, pbi = _pbr.ap(), _pbi.ap()
            pt = pbr[:, 0:128]
            pH = pbi[0:1, 0:1024]

            tld_p = S["ld"].mark(5 * 16)

            def p_ld(s, tld_p=tld_p):
                ap_ph = AP(phazor.tensor, 0, [[2, 128], [256, 8], [1, 2]])
                ap_pi = AP(phazor_init.tensor, 0, [[2, 128], [256, 8], [1, 2]])
                ap_lr = AP(last_re.tensor, 0, [[1, 128], [128, 8]])
                ap_li = AP(last_im.tensor, 0, [[1, 128], [128, 8]])
                s.dma_start(out=phin, in_=ap_ph).then_inc(S["ld"].h, 16)
                s.dma_start(out=phii, in_=ap_pi).then_inc(S["ld"].h, 16)
                s.dma_start(out=lastv[:, :, 0], in_=ap_lr, allow_slow_non_contiguous=True).then_inc(S["ld"].h, 16)
                s.dma_start(out=lastv[:, :, 1], in_=ap_li, allow_slow_non_contiguous=True).then_inc(S["ld"].h, 16)
                s.dma_start(out=hstg, in_=last_re[None, 0:1024]).then_inc(S["ld"].h, 16)
            E.sync(p_ld)

            tve_pw = S["ve"].mark()
            tac_pw = S["ac"].mark(2)

            def p_mag(v, tld_p=tld_p):
                v.wait_ge(S["ld"].h, tld_p)
                re, im = phin[:, :, 0:1], phin[:, :, 1:2]
                v.tensor_mul(mg[:, :, 0:1], re, re)
                v.scalar_tensor_tensor(mg[:, :, 4:5], im, 1.0, im, ALU.mult, ALU.mult)
                v.drain()
                v.tensor_add(mg[:, :, 0:1], mg[:, :, 0:1], mg[:, :, 4:5])
                v.drain().then_inc(S["ve"].h, 1)
            E.vector(p_mag)

            def p_sqrt(a, tve_pw=tve_pw):
                a.wait_ge(S["ve"].h, tve_pw)
                a.activation(mg[:, :, 1:2], mg[:, :, 0:1], AFT.Sqrt)
                a.drain().then_inc(S["ac"].h, 1)
            E.scalar(p_sqrt)

            tve_nw = S["ve"].mark()

            def p_newton(v, tac_pw=tac_pw):
                v.wait_ge(S["ac"].h, tac_pw - 1)
                v.reciprocal(mg[:, :, 2:3], mg[:, :, 1:2])
                v.drain()
                v.tensor_mul(mg[:, :, 4:5], mg[:, :, 0:1], mg[:, :, 2:3])
                v.drain()
                v.tensor_add(mg[:, :, 4:5], mg[:, :, 4:5], mg[:, :, 1:2])
                v.drain()
                v.tensor_scalar(mg[:, :, 1:2], mg[:, :, 4:5], 0.5, None, ALU.mult)
                v.drain()
                v.reciprocal(mg[:, :, 2:3], mg[:, :, 1:2])
                v.drain().then_inc(S["ve"].h, 1)
            E.vector(p_newton)

            def p_exp(a, tve_nw=tve_nw):
                a.wait_ge(S["ve"].h, tve_nw)
                a.activation(mg[:, :, 3:4], mg[:, :, 1:2], AFT.Exp, scale=-1.0)
                a.drain().then_inc(S["ac"].h, 1)
            E.scalar(p_exp)

            tve_pw2 = S["ve"].mark()

            def p_powers(v, tac_pw=tac_pw):
                v.wait_ge(S["ac"].h, tac_pw)
                v.tensor_mul(mg[:, :, 4:5], mg[:, :, 2:3], mg[:, :, 3:4])
                v.drain()
                for c in range(2):
                    P = (Pre, Pim)[c]
                    v.tensor_mul(P[:, :, 1:2], phin[:, :, c:c + 1], mg[:, :, 4:5])
                v.memset(Pre[:, :, 0:1], 1.0)
                v.memset(Pim[:, :, 0:1], 0.0)
                v.drain()
                k = 1
                while k < 128:
                    kk = min(k, 129 - (k + 1))
                    rek = Pre[:, :, k:k + 1].broadcast_to([128, 8, kk])
                    imk = Pim[:, :, k:k + 1].broadcast_to([128, 8, kk])
                    reA = Pre[:, :, 1:1 + kk]
                    imA = Pim[:, :, 1:1 + kk]
                    v.tensor_mul(tmA[:, 0:8 * kk].rearrange("p (a b) -> p a b", a=8), reA, rek)
                    v.tensor_mul(tmB[:, 0:8 * kk].rearrange("p (a b) -> p a b", a=8), imA, imk)
                    v.drain()
                    v.tensor_sub(Pre[:, :, k + 1:k + 1 + kk],
                                 tmA[:, 0:8 * kk].rearrange("p (a b) -> p a b", a=8),
                                 tmB[:, 0:8 * kk].rearrange("p (a b) -> p a b", a=8))
                    v.tensor_mul(tmA[:, 0:8 * kk].rearrange("p (a b) -> p a b", a=8), reA, imk)
                    v.tensor_mul(tmB[:, 0:8 * kk].rearrange("p (a b) -> p a b", a=8), imA, rek)
                    v.drain()
                    v.tensor_add(Pim[:, :, k + 1:k + 1 + kk],
                                 tmA[:, 0:8 * kk].rearrange("p (a b) -> p a b", a=8),
                                 tmB[:, 0:8 * kk].rearrange("p (a b) -> p a b", a=8))
                    v.drain()
                    k = 2 * k
                ir, ii = phii[:, :, 0:1], phii[:, :, 1:2]
                irB = ir.broadcast_to([128, 8, 129])
                iiB = ii.broadcast_to([128, 8, 129])
                v.tensor_mul(tmA[:, 0:1032].rearrange("p (a b) -> p a b", a=8), Pre[:, :, 0:129], irB)
                v.tensor_mul(tmB[:, 0:1032].rearrange("p (a b) -> p a b", a=8), Pim[:, :, 0:129], iiB)
                v.drain()
                v.tensor_sub(Are, tmA[:, 0:1032].rearrange("p (a b) -> p a b", a=8),
                             tmB[:, 0:1032].rearrange("p (a b) -> p a b", a=8))
                v.tensor_mul(tmA[:, 0:1032].rearrange("p (a b) -> p a b", a=8), Pre[:, :, 0:129], iiB)
                v.tensor_mul(tmB[:, 0:1032].rearrange("p (a b) -> p a b", a=8), Pim[:, :, 0:129], irB)
                v.drain()
                v.tensor_add(Aim, tmA[:, 0:1032].rearrange("p (a b) -> p a b", a=8),
                             tmB[:, 0:1032].rearrange("p (a b) -> p a b", a=8))
                v.drain()
                v.memset(Aext, 0.0)
                v.drain()
                v.tensor_copy(Aext[:, :, 127:256], Are)
                v.drain().then_inc(S["ve"].h, 1)
            E.vector(p_powers)

            tgs_p = S["gs"].mark(16 * 128)

            def p_st(g, tve_pw2=tve_pw2, tgs_p=tgs_p):
                g.wait_ge(S["ve"].h, tve_pw2)
                for r in range(128):
                    dsta = AP(powers_hbm.tensor, r * 256,
                              [[128 * 256, 128], [128 * 128 * 256, 8], [1, 256]])
                    g.dma_start(out=dsta, in_=Aext).then_inc(S["gs"].h, 16)
            E.gpsimd(p_st)

            prev_ev = None
            for dch in range(8):
                tpe_t = S["pe"].mark()

                def b_tr(t, dch=dch, prev_ev=prev_ev, tve_now=S["ve"].n):
                    if prev_ev is None:
                        t.wait_ge(S["ve"].h, tve_now)
                    else:
                        t.wait_ge(S["ve"].h, prev_ev)
                    t.transpose(pt, Aim[:, dch, 0:128], identF)
                    t.drain().then_inc(S["pe"].h, 1)
                E.tensor(b_tr)

                tve_t = S["ve"].mark()

                def b_ev(v, tpe_t=tpe_t):
                    v.wait_ge(S["pe"].h, tpe_t)
                    v.tensor_copy(t128, pt)
                    v.drain().then_inc(S["ve"].h, 1)
                E.vector(b_ev)

                tpe_j = S["pe"].mark()

                def b_j(t, tve_t=tve_t):
                    t.wait_ge(S["ve"].h, tve_t)
                    t.ldweights(Janti)
                    t.matmul(pt, Janti, t128, start=True, stop=True)
                    t.drain().then_inc(S["pe"].h, 1)
                E.tensor(b_j)

                tve_j = S["ve"].mark()

                def b_jev(v, dch=dch, tpe_j=tpe_j):
                    v.wait_ge(S["pe"].h, tpe_j)
                    v.tensor_copy(PimF[:, dch * 128:(dch + 1) * 128], pt)
                    v.drain().then_inc(S["ve"].h, 1)
                E.vector(b_jev)

                prev_ev = tve_j
                for c, dest in ((0, phpre), (1, phpim)):
                    P = (Pre, Pim)[c]
                    tpe_p = S["pe"].mark()

                    def b_php(t, P=P, dch=dch, prev_ev=prev_ev):
                        t.wait_ge(S["ve"].h, prev_ev)
                        t.transpose(pt, P[:, dch, 1:129], identF)
                        t.drain().then_inc(S["pe"].h, 1)
                    E.tensor(b_php)

                    tve_p = S["ve"].mark()

                    def b_phev(v, dest=dest, dch=dch, tpe_p=tpe_p):
                        v.wait_ge(S["pe"].h, tpe_p)
                        v.tensor_copy(dest[:, dch * 128:(dch + 1) * 128], pt)
                        v.drain().then_inc(S["ve"].h, 1)
                    E.vector(b_phev)
                    prev_ev = tve_p

            for n in range(NT):
                buf = n % 2
                need_ve = S["ve"].n if n >= 2 else 0
                tld = S["ld"].mark(16)

                def x_ld(s, n=n, buf=buf, need_ve=need_ve, tld=tld):
                    if need_ve:
                        s.wait_ge(S["ve"].h, need_ve)
                    s.dma_start(out=xt2[:, buf, :],
                                in_=src[n * 128:(n + 1) * 128, :]).then_inc(S["ld"].h, 16)
                E.sync(x_ld)

                tve_s1 = S["ve"].mark()

                def b_ln1(v, buf=buf, tld=tld):
                    v.wait_ge(S["ld"].h, tld)
                    ln_chain(v, None, xt2[:, buf, :], None, ln_sc, "v1")
                    v.drain().then_inc(S["ve"].h, 1)
                E.vector(b_ln1)

                tac_s = S["ac"].mark()

                def b_ln2(a, tve_s1=tve_s1):
                    a.wait_ge(S["ve"].h, tve_s1)
                    ln_chain(None, a, None, None, ln_sc, "a")
                    a.drain().then_inc(S["ac"].h, 1)
                E.scalar(b_ln2)

                tve_s2 = S["ve"].mark()

                def b_ln3(v, n=n, buf=buf, tac_s=tac_s):
                    v.wait_ge(S["ac"].h, tac_s)
                    ln_chain(v, None, xt2[:, buf, :], x3[:, n, :], ln_sc, "v2")
                    v.drain().then_inc(S["ve"].h, 1)
                E.vector(b_ln3)

            for n in range(NT):
                tve_h = S["ve"].mark()

                def h_mul(v, n=n):
                    v.tensor_tensor(tmC, x3[:, n, :], PimF, ALU.mult)
                    v.drain().then_inc(S["ve"].h, 1)
                E.vector(h_mul)

                tpe_h = S["pe"].mark()

                def h_mm(t, n=n, tve_h=tve_h):
                    t.wait_ge(S["ve"].h, tve_h)
                    t.ldweights(ones_col)
                    t.matmul(pH[:, 0:512], ones_col, tmC[:, 0:512],
                             start=True, stop=True)
                    t.matmul(pH[:, 512:1024], ones_col, tmC[:, 512:1024],
                             start=True, stop=True)
                    t.drain().then_inc(S["pe"].h, 1)
                E.tensor(h_mm)

                tgs_prev_h = S["gs"].n
                tve_h2 = S["ve"].mark()

                def h_ev(v, n=n, tpe_h=tpe_h, tgs_prev_h=tgs_prev_h):
                    v.wait_ge(S["pe"].h, tpe_h)
                    v.wait_ge(S["gs"].h, tgs_prev_h)
                    v.tensor_copy(hstg, pH)
                    v.drain().then_inc(S["ve"].h, 1)
                E.vector(h_ev)

                tgs_h = S["gs"].mark(16)

                def h_st(g, n=n, tve_h2=tve_h2):
                    g.wait_ge(S["ve"].h, tve_h2)
                    g.dma_start(out=H_hbm[1, n, :][None, :],
                                in_=hstg).then_inc(S["gs"].h, 16)
                E.gpsimd(h_st)

            tb_units = []
            for b32 in range(32):
                wbuf = b32 % 2
                need_pe = tb_units[b32 - 2][0] if b32 >= 2 else 0
                tgl_b = S["gl"].mark(16)

                def t_ld(g, b32=b32, wbuf=wbuf, need_pe=need_pe, tgl_b=tgl_b,
                         tgs_p=tgs_p):
                    if b32 == 0:
                        g.wait_ge(S["gs"].h, tgs_p)
                    if need_pe:
                        g.wait_ge(S["pe"].h, need_pe)
                    srct = AP(powers_hbm.tensor, 127 + b32 * 32 * 128 * 256,
                              [[255, 128], [128 * 256, 32], [1, 128]])
                    g.dma_start(out=Tb[:, wbuf, :, :], in_=srct).then_inc(S["gl"].h, 16)
                E.gpsimd(t_ld)

                tpe_last = None
                for c in range(32):
                    ch = b32 * 32 + c
                    par = (b32 * 32 + c) % 2
                    prev_ve = S["ve"].n
                    tpe_c = S["pe"].mark()

                    def p1_pe(t, b32=b32, c=c, ch=ch, par=par, wbuf=wbuf,
                              tgl_b=tgl_b, prev_ve=prev_ve,
                              tve_ln=S["ve"].n if (b32 == 0 and c == 0) else 0):
                        if c == 0:
                            t.wait_ge(S["gl"].h, tgl_b)
                        if b32 == 0 and c == 0:
                            t.wait_ge(S["ve"].h, tve_ln)
                        elif c >= 2 or b32 > 0:
                            t.wait_ge(S["ve"].h, prev_ve)
                        t.ldweights(Tb[:, wbuf, c, :])
                        mov = x3[:, :, ch]
                        t.matmul(pc[par], Tb[:, wbuf, c, :], mov,
                                 start=True, stop=True)
                        t.drain().then_inc(S["pe"].h, 1)
                    E.tensor(p1_pe)

                    tve_c = S["ve"].mark()

                    def p1_ev(v, ch=ch, par=par, tpe_c=tpe_c):
                        v.wait_ge(S["pe"].h, tpe_c)
                        v.tensor_copy(x3[:, :, ch], pc[par])
                        v.drain().then_inc(S["ve"].h, 1)
                    E.vector(p1_ev)
                tb_units.append((tpe_c, tve_c))

            tgs_hr = S["gs"].mark(16)

            def hr_st(g, tve_p1=S["ve"].n, tgs_hr=tgs_hr):
                g.wait_ge(S["ve"].h, tve_p1)
                g.dma_start(out=H_hbm[0:1, :, :], in_=x3[127:128, :, :]).then_inc(S["gs"].h, 16)
            E.gpsimd(hr_st)

            tld_h = S["ld"].mark(16)

            def h_ld(s, tgs_hr=tgs_hr, tld_h=tld_h):
                s.wait_ge(S["gs"].h, tgs_hr)
                aph = AP(H_hbm.tensor, 0, [[1024, 32], [32 * 1024, 2], [1, 1024]])
                s.dma_start(out=Hstage, in_=aph).then_inc(S["ld"].h, 16)
            E.sync(h_ld)

            for comp in range(2):
                for dch in range(8):
                    par = (comp * 8 + dch) % 2
                    prev_ve = S["ve"].n
                    tpe_ht = S["pe"].mark()

                    def ht_tr(t, comp=comp, dch=dch, par=par, tld_h=tld_h,
                              prev_ve=prev_ve, first=(comp == 0 and dch == 0)):
                        if first:
                            t.wait_ge(S["ld"].h, tld_h)
                        else:
                            t.wait_ge(S["ve"].h, prev_ve)
                        t.transpose(pc[par][:, 0:32],
                                    Hstage[:, comp, dch * 128:(dch + 1) * 128],
                                    identF[0:32, 0:32])
                        t.drain().then_inc(S["pe"].h, 1)
                    E.tensor(ht_tr)

                    tve_ht = S["ve"].mark()

                    def ht_ev(v, comp=comp, dch=dch, par=par, tpe_ht=tpe_ht):
                        v.wait_ge(S["pe"].h, tpe_ht)
                        dest = (Hre, Him)[comp]
                        v.tensor_copy(dest[:, dch, :], pc[par][:, 0:32])
                        v.drain().then_inc(S["ve"].h, 1)
                    E.vector(ht_ev)

            tve_rec = S["ve"].mark()

            def recur(v, tld_h=tld_h):
                v.wait_ge(S["ld"].h, tld_h)
                v.tensor_copy(Sre[:, :, 0:1], lastv[:, :, 0:1])
                v.tensor_copy(Sim[:, :, 0:1], lastv[:, :, 1:2])
                v.drain()
                cr = Pre[:, :, 128:129]
                ci = Pim[:, :, 128:129]
                for n in range(32):
                    if n == 16:
                        v.tensor_copy(Sre[:, :, 16:17], lastv[:, :, 0:1])
                        v.tensor_copy(Sim[:, :, 16:17], lastv[:, :, 1:2])
                        v.drain()
                    v.tensor_mul(mg[:, :, 0:1], cr, Sre[:, :, n:n + 1])
                    v.tensor_mul(mg[:, :, 1:2], ci, Sim[:, :, n:n + 1])
                    v.tensor_mul(mg[:, :, 2:3], cr, Sim[:, :, n:n + 1])
                    v.tensor_mul(mg[:, :, 3:4], ci, Sre[:, :, n:n + 1])
                    v.drain()
                    v.tensor_sub(mg[:, :, 4:5], mg[:, :, 0:1], mg[:, :, 1:2])
                    v.tensor_add(mg[:, :, 5:6], mg[:, :, 2:3], mg[:, :, 3:4])
                    v.drain()
                    v.tensor_add(Sre[:, :, n + 1:n + 2], mg[:, :, 4:5],
                                 Hre[:, :, n:n + 1])
                    v.tensor_add(Sim[:, :, n + 1:n + 2], mg[:, :, 5:6],
                                 Him[:, :, n:n + 1])
                    v.drain()
                v.drain().then_inc(S["ve"].h, 1)
            E.vector(recur)

            for comp in range(2):
                for dch in range(8):
                    prev_ve = S["ve"].n
                    tpe_st = S["pe"].mark()

                    def st_tr(t, comp=comp, dch=dch, tve_rec=tve_rec,
                              prev_ve=prev_ve, first=(comp == 0 and dch == 0)):
                        if first:
                            t.wait_ge(S["ve"].h, tve_rec)
                        else:
                            t.wait_ge(S["ve"].h, prev_ve)
                        Ssrc = (Sre, Sim)[comp]
                        t.transpose(pbr[0:33, 0:128], Ssrc[:, dch, 0:33], identF)
                        t.drain().then_inc(S["pe"].h, 1)
                    E.tensor(st_tr)

                    tve_st = S["ve"].mark()

                    def st_ev(v, comp=comp, dch=dch, tpe_st=tpe_st):
                        v.wait_ge(S["pe"].h, tpe_st)
                        v.tensor_copy(St[:, comp, dch, :], pbr[0:33, 0:128])
                        v.drain().then_inc(S["ve"].h, 1)
                    E.vector(st_ev)

            tgs_s = S["gs"].mark(16)

            def s_st(g, tve_s_all=S["ve"].n, tgs_s=tgs_s):
                g.wait_ge(S["ve"].h, tve_s_all)
                aps = AP(S_hbm.tensor, 0,
                         [[1024, 33], [33 * 1024, 2], [128, 8], [1, 128]])
                g.dma_start(out=aps, in_=St).then_inc(S["gs"].h, 16)
            E.gpsimd(s_st)

            for n in range(NT):
                buf = n % 2
                need_ve = S["ve"].n if n >= 2 else 0
                tgl_s = S["gl"].mark(16)

                def s_ld(g, n=n, buf=buf, need_ve=need_ve, tgl_s=tgl_s,
                         tgs_s=tgs_s):
                    if n == 0:
                        g.wait_ge(S["gs"].h, tgs_s)
                    if need_ve:
                        g.wait_ge(S["ve"].h, need_ve)
                    aps = AP(S_hbm.tensor, n * D, [[1, 1], [33 * D, 2], [1, D]])
                    g.dma_start(out=srow[:, buf, :, :], in_=aps).then_inc(S["gl"].h, 16)
                E.gpsimd(s_ld)

                prev_ve2 = S["ve"].n
                tpe_b2 = S["pe"].mark()

                def c_mm(t, n=n, buf=buf, tgl_s=tgl_s, prev_ve2=prev_ve2):
                    t.wait_ge(S["gl"].h, tgl_s)
                    if n >= 1:
                        t.wait_ge(S["ve"].h, prev_ve2)
                    t.ldweights(ones_row)
                    for q in range(2):
                        t.matmul(pbr[:, q * 512:(q + 1) * 512], ones_row,
                                 srow[:, buf, 0, q * 512:(q + 1) * 512],
                                 start=True, stop=True)
                        t.matmul(pbi[:, q * 512:(q + 1) * 512], ones_row,
                                 srow[:, buf, 1, q * 512:(q + 1) * 512],
                                 start=True, stop=True)
                    t.drain().then_inc(S["pe"].h, 1)
                E.tensor(c_mm)

                tve_c2 = S["ve"].mark()

                def c_ve(v, n=n, tpe_b2=tpe_b2):
                    v.wait_ge(S["pe"].h, tpe_b2)
                    v.tensor_tensor(tmA[:, 0:1024], phpre, pbr, ALU.mult)
                    v.tensor_tensor(tmB[:, 0:1024], phpim, pbi, ALU.mult)
                    v.drain()
                    v.tensor_sub(tmC, tmA[:, 0:1024], tmB[:, 0:1024])
                    v.drain()
                    v.tensor_add(x3[:, n, :], x3[:, n, :], tmC)
                    v.drain().then_inc(S["ve"].h, 1)
                E.vector(c_ve)

            for n in range(NT):
                buf = n % 2
                tst_prev = S["st"].n if n >= 2 else 0
                tld = S["ld"].mark(16)

                def f_ld(s, n=n, buf=buf, tst_prev=tst_prev, tld=tld):
                    if tst_prev:
                        s.wait_ge(S["st"].h, tst_prev)
                    s.dma_start(out=rt2[:, buf, :],
                                in_=src[n * 128:(n + 1) * 128, :]).then_inc(S["ld"].h, 16)
                E.sync(f_ld)

                tve_c2_all = S["ve"].n
                tve_f = S["ve"].mark()

                def f_add(v, n=n, buf=buf, tld=tld, tve_c2_all=tve_c2_all):
                    v.wait_ge(S["ld"].h, tld)
                    if n == 0:
                        v.wait_ge(S["ve"].h, tve_c2_all)
                    v.tensor_tensor(rt2[:, buf, :], rt2[:, buf, :],
                                    x3[:, n, :], ALU.add)
                    v.drain().then_inc(S["ve"].h, 1)
                E.vector(f_add)

                tst = S["st"].mark(16)

                def f_st(s, n=n, buf=buf, tve_f=tve_f):
                    s.wait_ge(S["ve"].h, tve_f)
                    s.dma_start(out=dst[n * 128:(n + 1) * 128, :],
                                in_=rt2[:, buf, :]).then_inc(S["st"].h, 16)
                E.sync(f_st)

            E.flush()


def kernel(L0_attn_a_query, L0_attn_a_key_in, L0_attn_a_value, L0_attn_a_qkv_w,
           L0_attn_a_qkv_b, L0_attn_a_off_w, L0_attn_a_lin2_w, L0_attn_a_lin2_b,
           L0_attn_b_phazor, L0_attn_b_phazor_init, L0_attn_b_last_conv_re,
           L0_attn_b_last_conv_im, L0_ffn_w_gate, L0_ffn_w_up, L0_ffn_w_down):
    x0 = np.ascontiguousarray(np.asarray(L0_attn_a_query, dtype=np.float32))
    in_common = dict(
        qkv_wT=np.ascontiguousarray(np.asarray(L0_attn_a_qkv_w, np.float32).T),
        qkv_b=np.ascontiguousarray(np.asarray(L0_attn_a_qkv_b, np.float32)),
        off_w=np.ascontiguousarray(np.asarray(L0_attn_a_off_w, np.float32)),
        lin2_wT=np.ascontiguousarray(np.asarray(L0_attn_a_lin2_w, np.float32).T),
        lin2_b=np.ascontiguousarray(np.asarray(L0_attn_a_lin2_b, np.float32)),
        phazor=np.ascontiguousarray(np.asarray(L0_attn_b_phazor, np.float32)),
        phazor_init=np.ascontiguousarray(np.asarray(L0_attn_b_phazor_init, np.float32)),
        last_re=np.ascontiguousarray(np.asarray(L0_attn_b_last_conv_re, np.float32)),
        last_im=np.ascontiguousarray(np.asarray(L0_attn_b_last_conv_im, np.float32)),
        wgT=np.ascontiguousarray(np.asarray(L0_ffn_w_gate, np.float32).T),
        wuT=np.ascontiguousarray(np.asarray(L0_ffn_w_up, np.float32).T),
        wdT=np.ascontiguousarray(np.asarray(L0_ffn_w_down, np.float32).T),
    )
    nc = build_program(N_SUB)
    in_maps = []
    for c in range(N_CORES):
        xs = np.ascontiguousarray(x0[c * BC:(c + 1) * BC].reshape(T, D))
        in_maps.append(dict(x0=xs, **in_common))
    res = run_bass_kernel_spmd(nc, in_maps, core_ids=list(range(N_CORES)))
    outs = [r["out"].reshape(BC, L, D) for r in res.results]
    return np.concatenate(outs, axis=0)
